# version 05 (final)
# speedup vs baseline: 1.1085x; 1.0592x over previous
import math
import contextlib
import numpy as np
import ml_dtypes
import concourse.bass as bass
import concourse.mybir as mybir
from concourse.bass_utils import run_bass_kernel_spmd

F32 = mybir.dt.float32
F16 = mybir.dt.float16
BF16 = mybir.dt.bfloat16
F8 = mybir.dt.float8e4
AF = mybir.ActivationFunctionType
OP = mybir.AluOpType
AX = mybir.AxisListType

NCORES = 8
SEQ, NSEQ = 2048, 2
TOK = SEQ * NSEQ
D = 1024
G = 512
NG_SEQ = SEQ // G
NG = TOK // G
FC = D // 128
NT_G = G // 128

H0, DF0, CHUNK = 8, 256, 64
NCH_G = G // CHUNK
RMS_EPS = 1e-5
LN_EPS = 1e-5
LN16 = math.log(16.0)

I0 = 2816
IC0 = I0 // 128
QMAX = 127.0

H1, DK1 = 16, 64
I1 = 4096
IC1 = I1 // 128

ENGS = ("sync", "tensor", "vector", "scalar", "gpsimd")


_TAP = [None]


def build_program(stage=4):
    nc = bass.Bass("TRN2", target_bir_lowering=False,
                   detect_race_conditions=False)
    cnt = {}
    nbar = [0]
    tap_state = {"hit": False}

    def inc(inst, s, n=1):
        inst.then_inc(s, n)
        cnt[s.name] = cnt.get(s.name, 0) + n
        return cnt[s.name]

    def bump(s, n=1):
        cnt[s.name] = cnt.get(s.name, 0) + n
        return cnt[s.name]

    def val(s):
        return cnt.get(s.name, 0)

    def step(**fns):
        with nc.Block() as blk:
            for name, fn in fns.items():
                getattr(blk, name)(fn)

    ein = lambda nm, sh, dt: nc.dram_tensor(nm, sh, dt, kind="ExternalInput").ap()
    dts = lambda nm, sh, dt: nc.dram_tensor(nm, sh, dt).ap()

    x_in = ein("x", [TOK, D], F32)
    wqt0 = ein("wqt0", [D, D], F32)
    wkt0 = ein("wkt0", [D, D], F32)
    wvt0 = ein("wvt0", [D, D], F32)
    wot0 = ein("wot0", [D, D], F32)
    fmqw_i = ein("fmqw", [128, 128], F32)
    fmkw_i = ein("fmkw", [128, 128], F32)
    fmqb_i = ein("fmqb", [128], F32)
    fmkb_i = ein("fmkb", [128], F32)
    rmsw_i = ein("rmsw", [128], F32)
    clipx_i = ein("clipx", [1], F32)
    clipd_i = ein("clipd", [1], F32)
    wg_n = ein("wg", [I0, D], F32)
    wu_n = ein("wu", [I0, D], F32)
    wd_n = ein("wd", [D, I0], F32)
    qwp_i = ein("qwp", [D, D], F32)
    kwp_i = ein("kwp", [D, D], F32)
    vwp_i = ein("vwp", [D, D], F32)
    owp_i = ein("owp", [D, D], F32)
    w1t_i = ein("w1t", [D, I1], F32)
    b1_i = ein("b1", [I1], F32)
    w2t_i = ein("w2t", [I1, D], F32)
    b2_i = ein("b2", [D], F32)
    id32_i = ein("id32", [128, 128], F32)
    id16_i = ein("id16", [128, 128], F16)
    idb16_i = ein("idb16", [128, 128], BF16)
    mask64_i = ein("mask64", [128, 64], F32)
    mask128_i = ein("mask128", [128, 128], F32)
    ones16_i = ein("ones16", [1, 128], F16)
    ones32_i = ein("ones32", [1, 128], F32)

    y_out = nc.dram_tensor("y", [TOK, D], F32, kind="ExternalOutput").ap()
    dbg_out = nc.dram_tensor("dbg", [128, 4096], F32, kind="ExternalOutput").ap()

    xb = [dts("xb0", [TOK, D], F32), dts("xb1", [TOK, D], F32)]
    wsplit = {}
    for nm, cols in [("wq", D), ("wk", D), ("wv", D), ("qw", D), ("kw", D),
                     ("vw", D), ("ow", D), ("w1", I1)]:
        wsplit[nm] = [dts(nm + "_hi", [D, cols], F16),
                      dts(nm + "_lo", [D, cols], F16)]
    w2q_d = dts("w2q", [I1, D], F8)
    wgq_d = dts("wgq", [D, I0], BF16)
    wuq_d = dts("wuq", [D, I0], BF16)
    wdq_d = dts("wdq", [I0, D], BF16)
    h_buf = dts("h_buf", [I0, TOK], F32)
    oT_d = [dts("oT_hi", [D, SEQ], F16), dts("oT_lo", [D, SEQ], F16)]
    row_d = dts("row_d", [FC, 128], F32)
    cc_in = dts("cc_in", [1, 16], F32)
    cc_out = dts("cc_out", [1, 16], F32)

    stk = contextlib.ExitStack()
    sem = lambda nm: stk.enter_context(nc.semaphore(nm))
    sY = sem("sY")
    sT = sem("sT")
    sV = sem("sV")
    sS = sem("sS")
    sG = sem("sG")
    sC = sem("sC")
    sB = sem("sB")

    def dma(eng, dst, src, s=None, slow=False):
        s = s or sY
        return inc(eng.dma_start(out=dst, in_=src,
                                 allow_slow_non_contiguous=slow), s, 16)

    def barrier():
        nbar[0] += 1
        tgt = 5 * nbar[0]
        wy, wg = val(sY), val(sG)
        with nc.Block() as blk:
            def ft(t):
                t.ldweights(c_id16[:, 0:1]).then_inc(sB, 1)
                t.wait_ge(sB, tgt)
                t.wait_ge(sY, wy)
                t.wait_ge(sG, wg)
            blk.tensor(ft)

            def fv(v):
                v.engine_nop().then_inc(sB, 1)
                v.wait_ge(sB, tgt)
                v.wait_ge(sY, wy)
                v.wait_ge(sG, wg)
            blk.vector(fv)

            def fs(s):
                s.copy(c_bar[0:1, 0:1], c_bar[0:1, 1:2]).then_inc(sB, 1)
                s.wait_ge(sB, tgt)
                s.wait_ge(sY, wy)
                s.wait_ge(sG, wg)
            blk.scalar(fs)

            def fy(y):
                y.sem_inc(sB, 1)
                y.wait_ge(sB, tgt)
                y.wait_ge(sY, wy)
                y.wait_ge(sG, wg)
            blk.sync(fy)

            def fg(g):
                g.sem_inc(sB, 1)
                g.wait_ge(sB, tgt)
                g.wait_ge(sY, wy)
                g.wait_ge(sG, wg)
            blk.gpsimd(fg)
        cnt[sB.name] = tgt

    def tap(tag, key, src_ap, prows, ncols, cast=False):
        if _TAP[0] is None or _TAP[0][0] != tag or tuple(_TAP[0][1:]) != tuple(key):
            return
        tap_state["hit"] = True
        wv, wt, ws = val(sV), val(sT), val(sS)

        def fy(eng):
            eng.wait_ge(sV, wv)
            eng.wait_ge(sT, wt)
            eng.wait_ge(sS, ws)
            dma(eng, dbg_out[0:prows, 0:ncols], src_ap,
                s=sG if cast else sY)
            eng.wait_ge(sG if cast else sY, val(sG if cast else sY))
        if cast:
            step(gpsimd=fy)
        else:
            step(sync=fy)
        barrier()

    def hrows(dram, r0, nr, c0, ncol):
        return dram[r0:r0 + nr, c0:c0 + ncol].rearrange("(c p) n -> p c n", p=128)

    def t3(t, w, n, off=0):
        return t[:, off:off + n * w].rearrange("p (c n) -> p c n", c=n)

    psb = lambda nm, sh, dt: stk.enter_context(nc.sbuf_tensor(nm, sh, dt))
    pps = lambda nm, sh, dt: stk.enter_context(nc.psum_tensor(nm, sh, dt))

    c_id32 = psb("c_id32", [128, 128], F32)
    c_id16 = psb("c_id16", [128, 128], F16)
    c_idb16 = psb("c_idb16", [128, 128], BF16)
    c_mask = psb("c_mask", [128, 64], F32)
    c_mask4 = psb("c_mask4", [128, 256], F32)
    c_mask128 = psb("c_mask128", [128, 128], F32)
    c_ones16 = psb("c_ones16", [1, 128], F16)
    c_ones32 = psb("c_ones32", [1, 128], F32)
    c_fmqw = psb("c_fmqw", [128, 128], F32)
    c_fmkw = psb("c_fmkw", [128, 128], F32)
    c_bias = psb("c_bias", [128, 8], F32)
    c_fmkb_row = psb("c_fmkb_row", [1, 128], F32)
    c_rmsw = psb("c_rmsw", [128, 1], F32)
    c_m = psb("c_m", [128, 32], F32)
    c_r = psb("c_r", [128, 32], F32)
    c_t1 = psb("c_t1", [128, 8], F32)
    c_t2 = psb("c_t2", [128, 8], F32)
    c_misc = psb("c_misc", [128, 16], F32)
    c_swg = psb("c_swg", [128, IC0], F32)
    c_swu = psb("c_swu", [128, IC0], F32)
    c_sgt = psb("c_sgt", [128, IC0], F32)
    c_sut = psb("c_sut", [128, IC0], F32)
    c_swd = psb("c_swd", [128, FC], F32)
    c_b1 = psb("c_b1", [128, IC1], F32)
    c_b2q = psb("c_b2q", [1, D], F16)
    c_swd_b = psb("c_swd_b", [128, D], F32)
    c_rows = psb("c_rows", [8, 1024], F32)
    c_one = psb("c_one", [1, 8], F32)
    c_eps = psb("c_eps", [128, 6], F32)
    c_bar = psb("c_bar", [1, 2], F32)
    c_fnc = psb("c_fnc", [128, 512], F32)
    c_red = psb("c_red", [128, 64], F32)
    c_lnx = psb("c_lnx", [128, 64], F32)

    bx = [psb("bx0", [128, NT_G * D], F32), psb("bx1", [128, NT_G * D], F32)]
    scrX = psb("scrX", [128, D], F32)
    scrH = psb("scrH", [128, 2 * D], F16)

    pb = [pps(f"pb{i}", [128, 512], F32) for i in range(6)]
    pt16 = pps("pt16", [128, 1024], F16)
    ptb16 = pps("ptb16", [128, 1024], BF16)

    def vfence(v):
        return inc(v.tensor_scalar(c_fnc[:, 0:512], c_fnc[:, 0:512], 1.0,
                                   None, OP.mult), sV, 1)

    def sgap(sc):
        sc.activation(c_fnc[:, 0:512], c_fnc[:, 0:512], AF.Copy)

    def sfence(sc):
        return inc(sc.activation(c_fnc[:, 0:512], c_fnc[:, 0:512], AF.Copy),
                   sS, 1)

    def emit_round(v, src, dst_bf, tf, uf, nb0, invs, pre_clip):
        if pre_clip:
            v.tensor_scalar(tf, src, invs, -128.49, OP.mult, OP.max)
            v.tensor_scalar(tf, tf, 127.49, 256.5, OP.min, OP.add)
        else:
            v.tensor_scalar(tf, src, invs, 256.5, OP.mult, OP.add)
        v.tensor_copy(nb0, tf)
        v.tensor_tensor(tf, tf, nb0, OP.subtract)
        v.scalar_tensor_tensor(uf, tf, 0.0, nb0, OP.is_ge, OP.add)
        v.scalar_tensor_tensor(uf, tf, 1.0, uf, OP.is_ge, OP.add)
        v.tensor_scalar(dst_bf, uf, 257.0, None, OP.subtract)

    def mm3(t, out_ap, sh, sl, mh, ml, first, last):
        t.matmul(out_ap, sh, mh, start=first, stop=False)
        t.matmul(out_ap, sh, ml, start=False, stop=False)
        return t.matmul(out_ap, sl, mh, start=False, stop=last)

    def ln_stats(xbuf, gi, xld):
        b0 = gi * NT_G
        wt, ws = val(sT), val(sS)

        def fv(v):
            v.wait_ge(sY, xld)
            v.wait_ge(sT, wt)
            v.wait_ge(sS, ws)
            for tt in range(NT_G):
                v.reduce_sum(c_red[:, tt * 8:tt * 8 + 1],
                             xbuf[:, tt * D:(tt + 1) * D], axis=AX.X)
                v.tensor_tensor(scrX[:, 0:D], xbuf[:, tt * D:(tt + 1) * D],
                                xbuf[:, tt * D:(tt + 1) * D], OP.mult)
                v.reduce_sum(c_red[:, 32 + tt * 8:32 + tt * 8 + 1],
                             scrX[:, 0:D], axis=AX.X)
            vfence(v)
        step(vector=fv)
        vdn = val(sV)

        def fs2(sc):
            sc.wait_ge(sV, vdn)
            for tt in range(NT_G):
                sc.activation(c_m[:, b0 + tt:b0 + tt + 1],
                              c_red[:, tt * 8:tt * 8 + 1],
                              AF.Copy, scale=1.0 / D)
            for tt in range(NT_G):
                sc.activation(c_lnx[:, tt * 8:tt * 8 + 1],
                              c_m[:, b0 + tt:b0 + tt + 1], AF.Square)
            for tt in range(NT_G):
                sc.activation(c_lnx[:, tt * 8 + 4:tt * 8 + 5],
                              c_lnx[:, tt * 8:tt * 8 + 1],
                              AF.Identity, scale=-1.0, bias=c_eps[:, 0:1])
            for tt in range(NT_G):
                sc.activation(c_lnx[:, tt * 8 + 2:tt * 8 + 3],
                              c_red[:, 32 + tt * 8:32 + tt * 8 + 1],
                              AF.Ln, scale=1.0 / D,
                              bias=c_lnx[:, tt * 8 + 4:tt * 8 + 5])
            for tt in range(NT_G):
                sc.activation(c_r[:, b0 + tt:b0 + tt + 1],
                              c_lnx[:, tt * 8 + 2:tt * 8 + 3],
                              AF.Exp, scale=-0.5)
            sfence(sc)
        step(scalar=fs2)
        return val(sS)

    def xn_tile(v, xbuf, gi, tt, dst):
        c = gi * NT_G + tt
        v.tensor_scalar(dst, xbuf[:, tt * D:(tt + 1) * D],
                        c_m[:, c:c + 1], c_r[:, c:c + 1], OP.subtract, OP.mult)

    def xnT_tile(xbuf, gi, tt, dTh, dTl, TTW, toff, rdy):
        def fv(v):
            v.wait_ge(sS, rdy)
            v.wait_ge(sT, val(sT))
            xn_tile(v, xbuf, gi, tt, scrX[:, 0:D])
            v.tensor_copy(scrH[:, 0:D], scrX[:, 0:D])
            inc(v.tensor_tensor(scrH[:, D:2 * D], scrX[:, 0:D],
                                scrH[:, 0:D], OP.subtract), sV, 1)
        step(vector=fv)
        vdn = val(sV)
        for half, dT in ((0, dTh), (1, dTl)):
            def ft(t, half=half):
                t.wait_ge(sV, val(sV))
                i = None
                for c in range(FC):
                    i = t.transpose(pt16[:, c * 128:(c + 1) * 128],
                                    scrH[:, half * D + c * 128:
                                         half * D + (c + 1) * 128],
                                    c_id16[:])
                inc(i, sT, 1)
            step(tensor=ft)
            tdn = val(sT)

            def fv2(v, dT=dT, tdn=tdn):
                v.wait_ge(sT, tdn)
                dview = dT.rearrange("p (c s) -> p c s", c=FC)
                v.tensor_copy(dview[:, :, toff:toff + 128],
                              t3(pt16, 128, FC))
                vfence(v)
            step(vector=fv2)
        return val(sV)

    def phase0():
        def f(y):
            dma(y, c_id32[:], id32_i)
            dma(y, c_id16[:], id16_i)
            dma(y, c_idb16[:], idb16_i)
            dma(y, c_mask[:], mask64_i)
            dma(y, c_mask128[:], mask128_i)
            dma(y, c_ones16[:], ones16_i)
            dma(y, c_ones32[:], ones32_i)
            dma(y, c_fmqw[:], fmqw_i)
            dma(y, c_fmkw[:], fmkw_i)
            dma(y, c_misc[:, 0:1], fmqb_i.rearrange("(p a) -> p a", a=1))
            dma(y, c_misc[:, 1:2], fmkb_i.rearrange("(p a) -> p a", a=1))
            dma(y, c_rmsw[:], rmsw_i.rearrange("(p a) -> p a", a=1))
            dma(y, c_fmkb_row[:], fmkb_i.rearrange("(a n) -> a n", a=1))
            dma(y, c_b1[:], b1_i.rearrange("(c p) -> p c", p=128), slow=True)
            dma(y, c_one[0:1, 0:1], clipx_i.rearrange("(a b) -> a b", a=1))
            dma(y, c_one[0:1, 1:2], clipd_i.rearrange("(a b) -> a b", a=1))
        step(sync=f)
        ld = val(sY)

        def fv(v):
            v.wait_ge(sY, ld)
            v.tensor_scalar(c_bias[:, 0:1], c_misc[:, 0:1], -LN16, None, OP.add)
            v.tensor_scalar(c_bias[:, 1:2], c_misc[:, 0:1], -1.0, -LN16,
                            OP.mult, OP.add)
            v.tensor_copy(c_bias[:, 2:3], c_misc[:, 1:2])
            v.memset(c_eps[:, 0:1], LN_EPS)
            v.memset(c_eps[:, 1:2], -4.0)
            v.memset(c_eps[:, 2:3], 1.0)
            v.memset(c_eps[:, 3:4], math.log(QMAX))
            v.memset(c_eps[:, 4:5], RMS_EPS * (2.0 ** -40))
            for _q in range(4):
                v.tensor_copy(c_mask4[:, _q * 64:(_q + 1) * 64], c_mask[:])
            inc(v.tensor_scalar(c_bias[:, 3:4], c_misc[:, 1:2], -1.0, None,
                                OP.mult), sV, 1)
        step(vector=fv)

        with nc.sbuf_tensor("p0A", [128, 4096], F32) as A, \
             nc.sbuf_tensor("p0B", [128, 4096], F32) as B, \
             nc.sbuf_tensor("p0S", [128, 8192], F16) as S16, \
             nc.sbuf_tensor("p0N", [128, 2 * I0], BF16) as NB, \
             nc.sbuf_tensor("p0W8", [128, D], F8) as W8, \
             nc.sbuf_tensor("p0R", [1, D], F32) as ROW:

            def split_weight(src, dst_hl, rows, ncols):
                for b in range(rows // 128):
                    for c0 in range(0, ncols, 4096):
                        w = min(4096, ncols - c0)
                        yv = val(sY)

                        def fy0(y):
                            y.wait_ge(sV, val(sV))
                            dma(y, A[:, 0:w],
                                src[b * 128:(b + 1) * 128, c0:c0 + w])
                        step(sync=fy0)
                        ld = val(sY)

                        def fv1(v):
                            v.wait_ge(sY, ld)
                            v.tensor_copy(S16[:, 0:w], A[:, 0:w])
                            inc(v.tensor_tensor(S16[:, w:2 * w], A[:, 0:w],
                                                S16[:, 0:w], OP.subtract),
                                sV, 1)
                        step(vector=fv1)
                        dn = val(sV)

                        def fy1(y):
                            y.wait_ge(sV, dn)
                            dma(y, dst_hl[0][b * 128:(b + 1) * 128, c0:c0 + w],
                                S16[:, 0:w])
                            dma(y, dst_hl[1][b * 128:(b + 1) * 128, c0:c0 + w],
                                S16[:, w:2 * w])
                        step(sync=fy1)

            def quant_rows(src, dst_T, scl, nblocks, ncols):
                for b in range(nblocks):
                    def fy0(y, b=b):
                        y.wait_ge(sV, val(sV))
                        dma(y, A[:, 0:ncols], src[b * 128:(b + 1) * 128, :])
                    step(sync=fy0)
                    ld = val(sY)

                    def fv1(v, b=b):
                        v.wait_ge(sY, ld)
                        v.wait_ge(sT, val(sT))
                        v.wait_ge(sS, val(sS))
                        v.reduce_max(c_red[:, 0:1], A[:, 0:ncols], axis=AX.X,
                                     apply_absolute_value=True)
                        vfence(v)
                    step(vector=fv1)
                    rdv = val(sV)

                    def fsc1(sc, b=b):
                        sc.wait_ge(sV, rdv)
                        sc.activation(scl[:, b:b + 1], c_red[:, 0:1],
                                      AF.Copy, scale=1.0 / QMAX)
                        sfence(sc)
                    step(scalar=fsc1)
                    sdv = val(sS)

                    def fv1b(v, b=b):
                        v.wait_ge(sS, sdv)
                        v.reciprocal(c_red[:, 8:9], c_red[:, 0:1])
                        k = vfence(v)
                        v.wait_ge(sV, k)
                        v.tensor_scalar(c_red[:, 16:17], c_red[:, 8:9],
                                        QMAX, None, OP.mult)
                        vfence(v)
                        v.wait_ge(sV, val(sV))
                        emit_round(v, A[:, 0:ncols], NB[:, 0:ncols],
                                   A[:, 0:ncols], B[:, 8:8 + ncols],
                                   NB[:, 0:ncols], c_red[:, 16:17],
                                   pre_clip=False)
                        inc(v.engine_nop(), sV, 1)
                    step(vector=fv1b)

                    nch = ncols // 128
                    for c0 in range(0, nch, 8):
                        cw = min(8, nch - c0)

                        def ft(t, c0=c0, cw=cw):
                            t.wait_ge(sV, val(sV))
                            i = None
                            for c in range(cw):
                                i = t.transpose(
                                    ptb16[:, c * 128:(c + 1) * 128],
                                    NB[:, (c0 + c) * 128:(c0 + c + 1) * 128],
                                    c_idb16[:])
                            inc(i, sT, 1)
                        step(tensor=ft)
                        tdn = val(sT)

                        def fv2(v, c0=c0, cw=cw, tdn=tdn):
                            v.wait_ge(sT, tdn)
                            v.tensor_copy(
                                NB[:, I0 + c0 * 128:I0 + (c0 + cw) * 128],
                                ptb16[:, 0:cw * 128])
                            vfence(v)
                        step(vector=fv2)
                        edn = val(sV)

                        def fy1(y, b=b, c0=c0, cw=cw, edn=edn):
                            y.wait_ge(sV, edn)
                            dma(y, hrows(dst_T, c0 * 128, cw * 128,
                                         b * 128, 128),
                                t3(NB, 128, cw, off=I0 + c0 * 128))
                        step(sync=fy1)

            split_weight(wqt0, wsplit["wq"], D, D)
            split_weight(wkt0, wsplit["wk"], D, D)
            split_weight(wvt0, wsplit["wv"], D, D)
            if stage >= 2:
                quant_rows(wg_n, wgq_d, c_swg, IC0, D)
                quant_rows(wu_n, wuq_d, c_swu, IC0, D)
                quant_rows(wd_n, wdq_d, c_swd, FC, I0)
                vd = val(sV)

                def ft1(t):
                    t.wait_ge(sV, vd)
                    inc(t.transpose(pb[0][0:FC, 0:128], c_swd[:, 0:FC],
                                    c_id32[:]), sT, 1)
                step(tensor=ft1)
                td = val(sT)

                def fv3(v):
                    v.wait_ge(sT, td)
                    inc(v.tensor_copy(c_rows[0:FC, 0:128],
                                      pb[0][0:FC, 0:128]), sV, 1)
                step(vector=fv3)
                vd = val(sV)

                def fy2(y):
                    y.wait_ge(sV, vd)
                    dma(y, row_d[:, :], c_rows[0:FC, 0:128])
                step(sync=fy2)
                yd = val(sY)

                def fy3(y):
                    y.wait_ge(sY, yd)
                    dma(y, ROW[:], row_d.rearrange("a b -> (a b)")
                        .rearrange("(a n) -> a n", a=1))
                step(sync=fy3)
                yd = val(sY)

                def ft2(t):
                    t.wait_ge(sY, yd)
                    t.matmul(pb[0][:, 0:512], c_ones32[:], ROW[0:1, 0:512],
                             start=True, stop=True)
                    inc(t.matmul(pb[1][:, 0:512], c_ones32[:],
                                 ROW[0:1, 512:1024], start=True, stop=True),
                        sT, 1)
                step(tensor=ft2)
                td = val(sT)

                def fv4(v):
                    v.wait_ge(sT, td)
                    v.tensor_copy(c_swd_b[:, 0:512], pb[0][:, 0:512])
                    inc(v.tensor_copy(c_swd_b[:, 512:1024], pb[1][:, 0:512]),
                        sV, 1)
                step(vector=fv4)
            if stage >= 3:
                split_weight(qwp_i, wsplit["qw"], D, D)
                split_weight(kwp_i, wsplit["kw"], D, D)
                split_weight(vwp_i, wsplit["vw"], D, D)
                split_weight(owp_i, wsplit["ow"], D, D)
            if stage >= 4:
                split_weight(w1t_i, wsplit["w1"], D, I1)
                for b in range(IC1):
                    def fy4(y, b=b):
                        y.wait_ge(sV, val(sV))
                        dma(y, A[:, 0:D], w2t_i[b * 128:(b + 1) * 128, :])
                    step(sync=fy4)
                    ld2 = val(sY)

                    def fv5(v, ld2=ld2):
                        v.wait_ge(sY, ld2)
                        inc(v.tensor_copy(W8[:], A[:, 0:D]), sV, 1)
                    step(vector=fv5)
                    dn = val(sV)

                    def fy5(y, b=b, dn=dn):
                        y.wait_ge(sV, dn)
                        dma(y, w2q_d[b * 128:(b + 1) * 128, :], W8[:])
                    step(sync=fy5)
                def fy6(y):
                    y.wait_ge(sV, val(sV))
                    dma(y, A[0:1, 0:D], b2_i.rearrange("(a n) -> a n", a=1))
                step(sync=fy6)
                ld3 = val(sY)

                def fv6(v):
                    v.wait_ge(sY, ld3)
                    inc(v.tensor_copy(c_b2q[:], A[0:1, 0:D]), sV, 1)
                step(vector=fv6)
            barrier()

    def phase_l0attn(xsrc, xdst):
        with nc.sbuf_tensor("a0_xnTh", [128, FC * G], F16) as xnTh0, \
             nc.sbuf_tensor("a0_xnTl", [128, FC * G], F16) as xnTl0, \
             nc.sbuf_tensor("a0_wq", [128, 4 * FC * 128], F16) as wq_s, \
             nc.sbuf_tensor("a0_wk", [128, 4 * FC * 128], F16) as wk_s, \
             nc.sbuf_tensor("a0_wv", [128, 4 * FC * 128], F16) as wv_s, \
             nc.sbuf_tensor("a0_pf", [128, G], F32) as b_pf, \
             nc.sbuf_tensor("a0_qF", [128, 2 * G], F32) as b_qF, \
             nc.sbuf_tensor("a0_kF", [128, 2 * G], F32) as b_kF, \
             nc.sbuf_tensor("a0_kiF", [128, NT_G * DF0], F32) as b_kiF, \
             nc.sbuf_tensor("a0_vh", [128, NT_G * 128], F32) as b_vh, \
             nc.sbuf_tensor("a0_oacc", [128, G], F32) as b_oacc, \
             nc.sbuf_tensor("a0_S", [128, H0 * DF0], F32) as b_S, \
             nc.sbuf_tensor("a0_aM", [128, NCH_G * 64], F32) as b_aM, \
             nc.sbuf_tensor("a0_oT", [128, H0 * G], F32) as b_oT, \
             nc.sbuf_tensor("a0_wo", [128, FC * G], F32) as b_wo:

            def wbuf(ws, half, buf):
                off = (half * 2 + buf) * FC * 128
                return ws[:, off:off + FC * 128]

            def load_head_w(y, h, buf):
                for ws, nm in ((wq_s, "wq"), (wk_s, "wk"), (wv_s, "wv")):
                    for half in range(2):
                        dma(y, t3(wbuf(ws, half, buf), 128, FC),
                            hrows(wsplit[nm][half], 0, D, h * 128, 128))

            head_mark = {}
            for s in range(NSEQ):
                step(vector=lambda v: [
                    v.wait_ge(sT, val(sT)),
                    inc(v.memset(b_S[:], 0.0), sV, 1)][-1])
                for g in range(NG_SEQ):
                    gi = s * NG_SEQ + g
                    xg = bx[gi % 2]

                    def fy(y, gi=gi, xg=xg):
                        y.wait_ge(sV, val(sV))
                        dma(y, t3(xg, D, NT_G), hrows(xsrc, gi * G, G, 0, D))
                        if gi == 0:
                            load_head_w(y, 0, 0)
                    step(sync=fy)
                    xld = val(sY)
                    tap("bx", (gi,), xg[:, 0:NT_G * D], 128, NT_G * D)
                    rdy = ln_stats(xg, gi, xld)
                    tap("m", (gi,), c_m[:, 0:32], 128, 32)
                    tap("r", (gi,), c_r[:, 0:32], 128, 32)
                    for tt in range(NT_G):
                        xnT_tile(xg, gi, tt, xnTh0, xnTl0, G, tt * 128, rdy)
                    xnT_dn = val(sV)
                    tap("xnTh", (gi,), xnTh0[:, 0:4096], 128, 4096, cast=True)

                    for h in range(H0):
                        ghead = gi * H0 + h
                        buf = h % 2

                        def fy2(y, gi=gi, h=h, ghead=ghead):
                            y.wait_ge(sT, head_mark.get(ghead - 1, 0))
                            if h + 1 < H0:
                                load_head_w(y, h + 1, (h + 1) % 2)
                            elif gi + 1 < NG:
                                load_head_w(y, 0, 0)
                        step(sync=fy2)
                        wrdy = val(sY)

                        def ftq(t, buf=buf):
                            t.wait_ge(sY, wrdy)
                            t.wait_ge(sV, xnT_dn)
                            i = None
                            for fc in range(FC):
                                i = mm3(t, pb[0][:, 0:G],
                                        wbuf(wq_s, 0, buf)[:, fc * 128:(fc + 1) * 128],
                                        wbuf(wq_s, 1, buf)[:, fc * 128:(fc + 1) * 128],
                                        xnTh0[:, fc * G:(fc + 1) * G],
                                        xnTl0[:, fc * G:(fc + 1) * G],
                                        fc == 0, fc == FC - 1)
                            inc(i, sT, 1)
                        step(tensor=ftq)
                        qmm = val(sT)
                        step(vector=lambda v, qmm=qmm: [
                            v.wait_ge(sT, qmm),
                            v.tensor_copy(b_pf[:], pb[0][:, 0:G]),
                            vfence(v)][-1])
                        qev = val(sV)
                        tap("qpf", (gi, h), b_pf[:, 0:G], 128, G)
                        step(tensor=lambda t, qev=qev: [
                            t.wait_ge(sV, qev),
                            inc(t.matmul(pb[1][:, 0:G], c_fmqw[:], b_pf[:],
                                         start=True, stop=True), sT, 1)][-1])
                        fmm = val(sT)
                        step(scalar=lambda sc, fmm=fmm: [
                            sc.wait_ge(sT, fmm),
                            sc.activation(b_qF[:, 0:G], pb[1][:, 0:G], AF.Exp,
                                          bias=c_bias[:, 0:1]),
                            inc(sc.activation(b_qF[:, G:2 * G], pb[1][:, 0:G],
                                              AF.Exp, scale=-1.0,
                                              bias=c_bias[:, 1:2]), sS, 1)][-1])
                        qFdn = val(sS)
                        tap("qF", (gi, h), b_qF[:, 0:2 * G], 128, 2 * G)

                        def ftk(t, buf=buf, qFdn=qFdn):
                            t.wait_ge(sS, qFdn)
                            i = None
                            for fc in range(FC):
                                i = mm3(t, pb[0][:, 0:G],
                                        wbuf(wk_s, 0, buf)[:, fc * 128:(fc + 1) * 128],
                                        wbuf(wk_s, 1, buf)[:, fc * 128:(fc + 1) * 128],
                                        xnTh0[:, fc * G:(fc + 1) * G],
                                        xnTl0[:, fc * G:(fc + 1) * G],
                                        fc == 0, fc == FC - 1)
                            inc(i, sT, 1)
                        step(tensor=ftk)
                        kmm = val(sT)
                        step(vector=lambda v, kmm=kmm: [
                            v.wait_ge(sT, kmm),
                            v.tensor_copy(b_pf[:], pb[0][:, 0:G]),
                            vfence(v)][-1])
                        kev = val(sV)
                        step(tensor=lambda t, kev=kev: [
                            t.wait_ge(sV, kev),
                            inc(t.matmul(pb[1][:, 0:G], c_fmkw[:], b_pf[:],
                                         start=True, stop=True), sT, 1)][-1])
                        fkm = val(sT)
                        step(scalar=lambda sc, fkm=fkm: [
                            sc.wait_ge(sT, fkm),
                            sc.activation(b_kF[:, 0:G], pb[1][:, 0:G], AF.Exp,
                                          bias=c_bias[:, 2:3]),
                            inc(sc.activation(b_kF[:, G:2 * G], pb[1][:, 0:G],
                                              AF.Exp, scale=-1.0,
                                              bias=c_bias[:, 3:4]), sS, 1)][-1])
                        kFdn = val(sS)

                        for tt in range(NT_G):
                            def ftki(t, tt=tt, kFdn=kFdn):
                                t.wait_ge(sS, val(sS) if tt else kFdn)
                                t.matmul(pb[2][:, 0:128],
                                         b_pf[:, tt * 128:(tt + 1) * 128],
                                         c_fmkw[:], start=True, stop=False)
                                inc(t.matmul(pb[2][:, 0:128], c_ones32[:],
                                             c_fmkb_row[:], start=False,
                                             stop=True), sT, 1)
                            step(tensor=ftki)
                            kim = val(sT)
                            step(scalar=lambda sc, tt=tt, kim=kim: [
                                sc.wait_ge(sT, kim),
                                sc.activation(b_kiF[:, tt * DF0:tt * DF0 + 128],
                                              pb[2][:, 0:128], AF.Exp),
                                sc.activation(
                                    b_kiF[:, tt * DF0 + 128:(tt + 1) * DF0],
                                    pb[2][:, 0:128], AF.Exp, scale=-1.0),
                                sfence(sc)][-1])
                        kidn = val(sS)
                        tap("kF", (gi, h), b_kF[:, 0:2 * G], 128, 2 * G)
                        tap("kiF", (gi, h), b_kiF[:, 0:NT_G * DF0], 128, NT_G * DF0)

                        def ftv(t, buf=buf, kidn=kidn):
                            t.wait_ge(sS, kidn)
                            i = None
                            for fc in range(FC):
                                i = mm3(t, pb[0][:, 0:G],
                                        wbuf(wv_s, 0, buf)[:, fc * 128:(fc + 1) * 128],
                                        wbuf(wv_s, 1, buf)[:, fc * 128:(fc + 1) * 128],
                                        xnTh0[:, fc * G:(fc + 1) * G],
                                        xnTl0[:, fc * G:(fc + 1) * G],
                                        fc == 0, fc == FC - 1)
                            inc(i, sT, 1)
                        step(tensor=ftv)
                        vmm = val(sT)
                        step(vector=lambda v, vmm=vmm: [
                            v.wait_ge(sT, vmm),
                            v.tensor_copy(b_pf[:], pb[0][:, 0:G]),
                            vfence(v)][-1])
                        vev = val(sV)

                        def ftv2(t, vev=vev):
                            t.wait_ge(sV, vev)
                            i = None
                            for tt in range(NT_G):
                                i = t.transpose(pb[3][:, tt * 128:(tt + 1) * 128],
                                                b_pf[:, tt * 128:(tt + 1) * 128],
                                                c_id32[:])
                            inc(i, sT, 1)
                        step(tensor=ftv2)
                        vtm = val(sT)
                        step(vector=lambda v, vtm=vtm: [
                            v.wait_ge(sT, vtm),
                            v.tensor_copy(b_vh[:], pb[3][:, 0:NT_G * 128]),
                            vfence(v)][-1])
                        vhdn = val(sV)
                        tap("vh", (gi, h), b_vh[:, 0:NT_G * 128], 128, NT_G * 128)

                        def fta_all(t, vhdn=vhdn):
                            t.wait_ge(sV, vhdn)
                            i = None
                            for c in range(NCH_G):
                                po = (c % 2) * 64
                                co = (c // 2) * 64
                                t.matmul(pb[4][po:po + 64, co:co + 64],
                                         b_kF[:, c * 64:(c + 1) * 64],
                                         b_qF[:, c * 64:(c + 1) * 64],
                                         start=True, stop=False,
                                         tile_position=(0, po))
                                i = t.matmul(pb[4][po:po + 64, co:co + 64],
                                             b_kF[:, G + c * 64:G + (c + 1) * 64],
                                             b_qF[:, G + c * 64:G + (c + 1) * 64],
                                             start=False, stop=True,
                                             tile_position=(0, po))
                            inc(i, sT, 1)
                        step(tensor=fta_all)
                        atm = val(sT)

                        def fmask(v, atm=atm):
                            v.wait_ge(sT, atm)
                            for par in range(2):
                                po = par * 64
                                dst = b_aM[po:po + 64, 0:512].rearrange(
                                    "p (c s) -> p c s", c=4)[
                                        :, :, par * 64:par * 64 + 64]
                                v.tensor_tensor(
                                    dst,
                                    pb[4][po:po + 64, 0:256]
                                    .rearrange("p (c s) -> p c s", c=4),
                                    c_mask4[po:po + 64, 0:256]
                                    .rearrange("p (c s) -> p c s", c=4),
                                    OP.mult)
                            vfence(v)
                        step(vector=fmask)
                        amdn0 = val(sV)

                        for c in range(NCH_G):
                            par = c % 2
                            slot = c // 2
                            po = par * 64

                            def ftb(t, c=c, par=par, slot=slot, po=po,
                                    amdn=amdn0, h=h):
                                t.wait_ge(sV, amdn if c == 0 else val(sV))
                                t.matmul(pb[5][po:po + 64,
                                               slot * 128:(slot + 1) * 128],
                                         b_aM[po:po + 64, c * 64:(c + 1) * 64],
                                         b_vh[po:po + 64,
                                              slot * 128:(slot + 1) * 128],
                                         start=True, stop=False,
                                         tile_position=(po, po))
                                for half in range(2):
                                    t.matmul(pb[5][po:po + 64,
                                                   slot * 128:(slot + 1) * 128],
                                             b_qF[:, half * G + c * 64:
                                                  half * G + (c + 1) * 64],
                                             b_S[:, h * DF0 + half * 128:
                                                 h * DF0 + (half + 1) * 128],
                                             start=False, stop=half == 1,
                                             tile_position=(0, po))
                                i = None
                                for half in range(2):
                                    i = t.matmul(
                                        pb[2][:, half * 128:(half + 1) * 128],
                                        b_kiF[po:po + 64,
                                              slot * DF0 + half * 128:
                                              slot * DF0 + (half + 1) * 128],
                                        b_vh[po:po + 64,
                                             slot * 128:(slot + 1) * 128],
                                        start=True, stop=True,
                                        tile_position=(po, 0))
                                inc(i, sT, 1)
                            step(tensor=ftb)
                            dsm = val(sT)
                            step(vector=lambda v, h=h, dsm=dsm: [
                                v.wait_ge(sT, dsm),
                                v.tensor_tensor(
                                    b_S[:, h * DF0:(h + 1) * DF0],
                                    b_S[:, h * DF0:(h + 1) * DF0],
                                    pb[2][:, 0:DF0], OP.add),
                                vfence(v)][-1])

                        omm = val(sT)
                        step(vector=lambda v, omm=omm: [
                            v.wait_ge(sT, omm),
                            inc(v.tensor_copy(b_oacc[:], pb[5][:, 0:G]),
                                sV, 1)][-1])
                        odn = val(sV)
                        tap("aM", (gi, h), b_aM[:, 0:NCH_G * 64], 128, NCH_G * 64)
                        tap("S", (gi, h), b_S[:, 0:H0 * DF0], 128, H0 * DF0)
                        tap("o0", (gi, h), b_oacc[:, 0:G], 128, G)
                        def frms_v(v, odn=odn):
                            v.wait_ge(sV, odn)
                            v.wait_ge(sS, val(sS))
                            for tt in range(NT_G):
                                v.tensor_tensor(scrX[:, 0:128],
                                                b_oacc[:, tt * 128:(tt + 1) * 128],
                                                b_oacc[:, tt * 128:(tt + 1) * 128],
                                                OP.mult)
                                v.reduce_sum(c_red[:, tt * 8:tt * 8 + 1],
                                             scrX[:, 0:128], axis=AX.X)
                            vfence(v)
                        step(vector=frms_v)
                        rqdn = val(sV)
                        def frms(sc, rqdn=rqdn):
                            sc.wait_ge(sV, rqdn)
                            for tt in range(NT_G):
                                sc.activation(c_lnx[:, 32 + tt * 8:33 + tt * 8],
                                              c_red[:, tt * 8:tt * 8 + 1],
                                              AF.Sqrt,
                                              scale=(2.0 ** -40) / 128.0,
                                              bias=c_eps[:, 4:5])
                            sfence(sc)
                        step(scalar=frms)
                        rmsd = val(sS)
                        tap("rmsq", (gi, h), c_red[:, 0:64], 128, 64)
                        tap("rmsl", (gi, h), c_lnx[:, 0:64], 128, 64)
                        step(vector=lambda v, rmsd=rmsd: [
                            v.wait_ge(sS, rmsd),
                            [v.reciprocal(c_lnx[:, 36 + tt * 8:37 + tt * 8],
                                          c_lnx[:, 32 + tt * 8:33 + tt * 8])
                             for tt in range(NT_G)],
                            vfence(v)][-1])
                        rrec = val(sV)
                        step(vector=lambda v, rrec=rrec: [
                            v.wait_ge(sV, rrec),
                            [v.tensor_scalar(b_oacc[:, tt * 128:(tt + 1) * 128],
                                             b_oacc[:, tt * 128:(tt + 1) * 128],
                                             c_lnx[:, 36 + tt * 8:37 + tt * 8],
                                             2.0 ** -20, OP.mult, OP.mult)
                             for tt in range(NT_G)],
                            vfence(v)][-1])
                        rmsv = val(sV)

                        def fto(t, rmsv=rmsv):
                            t.wait_ge(sV, rmsv)
                            i = None
                            for tt in range(NT_G):
                                i = t.transpose(pb[3][:, tt * 128:(tt + 1) * 128],
                                                b_oacc[:, tt * 128:(tt + 1) * 128],
                                                c_id32[:])
                            inc(i, sT, 1)
                        step(tensor=fto)
                        otm = val(sT)
                        step(vector=lambda v, h=h, otm=otm: [
                            v.wait_ge(sT, otm),
                            v.tensor_copy(b_oT[:, h * G:(h + 1) * G],
                                          pb[3][:, 0:G]),
                            vfence(v)][-1])
                        head_mark[ghead] = val(sT)
                        tap("orms", (gi, h), b_oacc[:, 0:G], 128, G)
                        tap("oT", (gi, h), b_oT[:, 0:H0 * G], 128, H0 * G)

                    for foh in range(2):
                        def fyw(y, foh=foh):
                            y.wait_ge(sT, val(sT))
                            dma(y, t3(b_wo, G, FC), hrows(wot0, 0, D,
                                                          foh * G, G))
                        step(sync=fyw)
                        wld = val(sY)
                        step(vector=lambda v, wld=wld: [
                            v.wait_ge(sY, wld),
                            [v.tensor_scalar(b_wo[:, fc * G:(fc + 1) * G],
                                             b_wo[:, fc * G:(fc + 1) * G],
                                             c_rmsw[:, 0:1], None, OP.mult)
                             for fc in range(FC - 1)],
                            inc(v.tensor_scalar(
                                b_wo[:, (FC - 1) * G:FC * G],
                                b_wo[:, (FC - 1) * G:FC * G],
                                c_rmsw[:, 0:1], None, OP.mult), sV, 1)][-1])
                        wrdy2 = val(sV)
                        for tt in range(NT_G):
                            pbk = pb[tt % 2]

                            def ftw(t, tt=tt, pbk=pbk, wrdy2=wrdy2):
                                t.wait_ge(sV, val(sV) if tt >= 2 else wrdy2)
                                i = None
                                for fc in range(FC):
                                    i = t.matmul(pbk[:, 0:G],
                                                 b_oT[:, fc * G + tt * 128:
                                                      fc * G + (tt + 1) * 128],
                                                 b_wo[:, fc * G:(fc + 1) * G],
                                                 start=fc == 0,
                                                 stop=fc == FC - 1)
                                inc(i, sT, 1)
                            step(tensor=ftw)
                            wmm = val(sT)
                            step(vector=lambda v, tt=tt, pbk=pbk, foh=foh,
                                 xg=xg, wmm=wmm: [
                                v.wait_ge(sT, wmm),
                                v.tensor_tensor(
                                    xg[:, tt * D + foh * G:
                                       tt * D + (foh + 1) * G],
                                    xg[:, tt * D + foh * G:
                                       tt * D + (foh + 1) * G],
                                    pbk[:, 0:G], OP.add),
                                vfence(v)][-1])
                    rdn = val(sV)
                    tap("x1", (gi,), xg[:, 0:NT_G * D], 128, NT_G * D)
                    step(sync=lambda y, gi=gi, xg=xg, rdn=rdn: [
                        y.wait_ge(sV, rdn),
                        dma(y, hrows(xdst, gi * G, G, 0, D),
                            t3(xg, D, NT_G))][-1])
            barrier()

    def phase_ffn0(xsrc, xdst):
        def collective_max(clip_col):
            md = val(sV)

            def ft(t):
                t.wait_ge(sV, md)
                inc(t.transpose(pb[0][0:1, 0:128], c_misc[:, 5:6], c_id32[:]),
                    sT, 1)
            step(tensor=ft)
            td = val(sT)

            def fv(v):
                v.wait_ge(sT, td)
                v.memset(c_rows[0:1, 128:144], 0.0)
                v.reduce_max(c_rows[0:1, 128:129], pb[0][0:1, 0:128],
                             axis=AX.X)
                vfence(v)
            step(vector=fv)
            vd = val(sV)
            step(sync=lambda y, vd=vd: [
                y.wait_ge(sV, vd),
                dma(y, cc_in[:, :], c_rows[0:1, 128:144])][-1])
            yd = val(sY)

            def fg(g):
                g.wait_ge(sY, yd)
                inc(g.collective_compute(
                    "AllReduce", OP.max, replica_groups=[list(range(NCORES))],
                    ins=[cc_in[:, :]], outs=[cc_out[:, :]]), sC, 1)
            step(gpsimd=fg)
            cd = val(sC)
            step(sync=lambda y, cd=cd: [
                y.wait_ge(sC, cd),
                dma(y, c_rows[0:1, 144:160], cc_out[:, :])][-1])
            yd = val(sY)

            def fsc(sc):
                sc.wait_ge(sY, yd)
                sc.activation(c_one[0:1, 2:3], clip_col, AF.Exp, scale=-1.0)
                sgap(sc)
                sc.activation(c_one[0:1, 3:4], c_one[0:1, 2:3], AF.Ln,
                              bias=c_eps[0:1, 2:3])
                sgap(sc)
                sc.activation(c_one[0:1, 4:5], c_one[0:1, 3:4], AF.Exp,
                              scale=-1.0)
                sfence(sc)
            step(scalar=fsc)
            sd = val(sS)

            def fv2(v):
                v.wait_ge(sS, sd)
                v.tensor_tensor(c_one[0:1, 5:6], c_one[0:1, 4:5],
                                c_rows[0:1, 144:145], OP.mult)
                vfence(v)
            step(vector=fv2)
            vd = val(sV)

            def fsc2(sc):
                sc.wait_ge(sV, vd)
                sc.activation(c_one[0:1, 6:7], c_one[0:1, 5:6], AF.Copy,
                              scale=1.0 / QMAX)
                sfence(sc)
            step(scalar=fsc2)
            sd = val(sS)

            def fv2b(v):
                v.wait_ge(sS, sd)
                v.reciprocal(c_one[0:1, 7:8], c_one[0:1, 6:7])
                vfence(v)
            step(vector=fv2b)
            sd2 = val(sV)

            def ft2(t):
                t.wait_ge(sS, sd)
                t.wait_ge(sV, sd2)
                inc(t.matmul(pb[0][:, 0:2], c_ones32[:],
                             c_one[0:1, 6:8], start=True, stop=True), sT, 1)
            step(tensor=ft2)
            td = val(sT)

            def fv3(v):
                v.wait_ge(sT, td)
                v.tensor_copy(c_misc[:, 7:8], pb[0][:, 0:1])
                v.tensor_copy(c_misc[:, 6:7], pb[0][:, 1:2])
                vfence(v)
            step(vector=fv3)

        step(vector=lambda v: inc(v.memset(c_misc[:, 5:6], 0.0), sV, 1))
        for gi in range(NG):
            xg = bx[gi % 2]
            step(sync=lambda y, gi=gi, xg=xg: [
                y.wait_ge(sV, val(sV)),
                dma(y, t3(xg, D, NT_G), hrows(xsrc, gi * G, G, 0, D))][-1])
            xld = val(sY)
            rdy = ln_stats(xg, gi, xld)

            def fmax(v, gi=gi, xg=xg, rdy=rdy):
                v.wait_ge(sS, rdy)
                v.wait_ge(sT, val(sT))
                for tt in range(NT_G):
                    xn_tile(v, xg, gi, tt, scrX[:, 0:D])
                    v.reduce_max(c_t2[:, tt:tt + 1], scrX[:, 0:D], axis=AX.X,
                                 apply_absolute_value=True)
                k = vfence(v)
                v.wait_ge(sV, k)
                v.reduce_max(c_t2[:, 4:5], c_t2[:, 0:NT_G], axis=AX.X)
                k = vfence(v)
                v.wait_ge(sV, k)
                inc(v.tensor_tensor(c_misc[:, 5:6], c_misc[:, 5:6],
                                    c_t2[:, 4:5], OP.max), sV, 1)
            step(vector=fmax)

        collective_max(c_one[0:1, 0:1])
        tap("sx", (0,), c_misc[:, 0:16], 128, 16)
        tap("swg", (0,), c_swg[:, 0:IC0], 128, IC0)
        step(vector=lambda v: [
            v.wait_ge(sV, val(sV)),
            v.tensor_scalar(c_sgt[:], c_swg[:], c_misc[:, 7:8], None, OP.mult),
            v.tensor_scalar(c_sut[:], c_swu[:], c_misc[:, 7:8], None, OP.mult),
            inc(v.memset(c_misc[:, 5:6], 0.0), sV, 1)][-1])

        with nc.sbuf_tensor("f0_wg", [128, FC * I0], BF16) as b_wgq, \
             nc.sbuf_tensor("f0_wu", [128, FC * I0], BF16) as b_wuq, \
             nc.sbuf_tensor("f0_nxT", [128, FC * G], BF16) as b_nxT, \
             nc.sbuf_tensor("f0_nb", [128, D], BF16) as b_nb, \
             nc.sbuf_tensor("f0_u", [128, D], F32) as b_u, \
             nc.sbuf_tensor("f0_sg", [128, G], F32) as b_sg, \
             nc.sbuf_tensor("f0_up", [128, G], F32) as b_up, \
             nc.sbuf_tensor("f0_hs", [128, 2 * 4 * G], F32) as b_hs:
            step(sync=lambda y: [
                dma(y, t3(b_wgq, I0, FC), hrows(wgq_d, 0, D, 0, I0)),
                dma(y, t3(b_wuq, I0, FC), hrows(wuq_d, 0, D, 0, I0))][-1])
            wld = val(sY)
            hs_mark = [0, 0]
            for gi in range(NG):
                xg = bx[gi % 2]
                step(sync=lambda y, gi=gi, xg=xg: [
                    y.wait_ge(sV, val(sV)),
                    dma(y, t3(xg, D, NT_G), hrows(xsrc, gi * G, G, 0, D))][-1])
                xld = val(sY)
                for tt in range(NT_G):
                    def fq(v, tt=tt, xg=xg, gi=gi, xld=xld):
                        v.wait_ge(sY, xld)
                        v.wait_ge(sT, val(sT))
                        xn_tile(v, xg, gi, tt, scrX[:, 0:D])
                        emit_round(v, scrX[:, 0:D], b_nb[:], scrX[:, 0:D],
                                   b_u[:], b_nb[:], c_misc[:, 6:7],
                                   pre_clip=True)
                        vfence(v)
                    step(vector=fq)

                    def ft(t, tt=tt):
                        t.wait_ge(sV, val(sV))
                        i = None
                        for c in range(FC):
                            i = t.transpose(ptb16[:, c * 128:(c + 1) * 128],
                                            b_nb[:, c * 128:(c + 1) * 128],
                                            c_idb16[:])
                        inc(i, sT, 1)
                    step(tensor=ft)
                    tdn = val(sT)
                    step(vector=lambda v, tt=tt, tdn=tdn: [
                        v.wait_ge(sT, tdn),
                        v.tensor_copy(
                            b_nxT.rearrange("p (c s) -> p c s", c=FC)
                            [:, :, tt * 128:(tt + 1) * 128],
                            t3(ptb16, 128, FC)),
                        vfence(v)][-1])
                nxdn = val(sV)
                tap("nx", (gi,), b_nxT[:, 0:FC * G], 128, FC * G, cast=True)

                for ic in range(IC0):
                    hb = (ic // 4) % 2
                    hcol = hb * 4 * G + (ic % 4) * G

                    def ftg(t, ic=ic, nxdn=nxdn, wld=wld):
                        if ic == 0:
                            t.wait_ge(sV, nxdn)
                            t.wait_ge(sY, wld)
                        else:
                            t.wait_ge(sS, val(sS))
                            t.wait_ge(sV, val(sV))
                        for fc in range(FC):
                            t.matmul(pb[0][:, 0:G],
                                     b_wgq[:, fc * I0 + ic * 128:
                                           fc * I0 + (ic + 1) * 128],
                                     b_nxT[:, fc * G:(fc + 1) * G],
                                     start=fc == 0, stop=fc == FC - 1)
                        i = None
                        for fc in range(FC):
                            i = t.matmul(pb[1][:, 0:G],
                                         b_wuq[:, fc * I0 + ic * 128:
                                               fc * I0 + (ic + 1) * 128],
                                         b_nxT[:, fc * G:(fc + 1) * G],
                                         start=fc == 0, stop=fc == FC - 1)
                        inc(i, sT, 1)
                    step(tensor=ftg)
                    gmm = val(sT)
                    step(scalar=lambda sc, ic=ic, gmm=gmm: [
                        sc.wait_ge(sT, gmm),
                        inc(sc.activation(b_sg[:], pb[0][:, 0:G], AF.Silu,
                                          scale=c_sgt[:, ic:ic + 1]),
                            sS, 1)][-1])
                    sdn = val(sS)

                    def fvu(v, ic=ic, hb=hb, hcol=hcol, gmm=gmm, sdn=sdn):
                        v.wait_ge(sT, gmm)
                        v.tensor_scalar(b_up[:], pb[1][:, 0:G],
                                        c_sut[:, ic:ic + 1], None, OP.mult)
                        vfence(v)
                        v.wait_ge(sS, sdn)
                        if ic % 4 == 0:
                            v.wait_ge(sY, hs_mark[hb])
                        v.tensor_tensor(b_hs[:, hcol:hcol + G], b_sg[:],
                                        b_up[:], OP.mult)
                        v.reduce_max(c_t2[:, 6:7], b_hs[:, hcol:hcol + G],
                                     axis=AX.X, apply_absolute_value=True)
                        k = vfence(v)
                        v.wait_ge(sV, k)
                        inc(v.tensor_tensor(c_misc[:, 5:6], c_misc[:, 5:6],
                                            c_t2[:, 6:7], OP.max), sV, 1)
                    step(vector=fvu)
                    hdn = val(sV)
                    if ic % 4 == 3 or ic == IC0 - 1:
                        nchk = (ic % 4) + 1
                        i0c = ic - nchk + 1

                        def fy(y, gi=gi, hb=hb, i0c=i0c, nchk=nchk, hdn=hdn):
                            y.wait_ge(sV, hdn)
                            dma(y, hrows(h_buf, i0c * 128, nchk * 128,
                                         gi * G, G),
                                t3(b_hs, G, nchk, off=hb * 4 * G))
                        step(sync=fy)
                        hs_mark[hb] = val(sY)
            barrier()

        collective_max(c_one[0:1, 1:2])
        tap("sh", (0,), c_misc[:, 0:16], 128, 16)
        shdn = val(sV)

        with nc.sbuf_tensor("f0_wd", [128, IC0 * D], BF16) as b_wdq, \
             nc.sbuf_tensor("f0_h", [128, 2 * IC0 * 128], F32) as b_h, \
             nc.sbuf_tensor("f0_nh", [128, 2 * IC0 * 128], BF16) as b_nh, \
             nc.sbuf_tensor("f0_u2", [128, IC0 * 128], F32) as b_u2, \
             nc.sbuf_tensor("f0_d", [128, G], F32) as b_do:
            step(sync=lambda y: dma(y, t3(b_wdq, D, IC0),
                                    hrows(wdq_d, 0, I0, 0, D)))
            wld = val(sY)
            v_mark = [0, 0]
            for tb in range(TOK // 128):
                hb = tb % 2
                xg = bx[hb]
                ho = hb * IC0 * 128

                def fy(y, tb=tb, hb=hb, xg=xg):
                    y.wait_ge(sV, v_mark[hb])
                    dma(y, t3(b_h, 128, IC0, off=hb * IC0 * 128),
                        hrows(h_buf, 0, I0, tb * 128, 128))
                    dma(y, t3(xg, D, 1), hrows(xsrc, tb * 128, 128, 0, D))
                step(sync=fy)
                hld = val(sY)

                def fq(v, hb=hb, ho=ho, hld=hld):
                    v.wait_ge(sY, hld)
                    v.wait_ge(sV, shdn)
                    v.wait_ge(sT, val(sT))
                    emit_round(v, b_h[:, ho:ho + IC0 * 128],
                               b_nh[:, ho:ho + IC0 * 128],
                               b_h[:, ho:ho + IC0 * 128], b_u2[:],
                               b_nh[:, ho:ho + IC0 * 128], c_misc[:, 6:7],
                               pre_clip=True)
                    vfence(v)
                step(vector=fq)
                qdn = val(sV)
                for foh in range(2):
                    pbk = pb[foh]

                    def ft(t, hb=hb, ho=ho, foh=foh, pbk=pbk, qdn=qdn, wld=wld):
                        t.wait_ge(sV, val(sV) if foh else qdn)
                        t.wait_ge(sY, wld)
                        i = None
                        for ic in range(IC0):
                            i = t.matmul(pbk[:, 0:G],
                                         b_nh[:, ho + ic * 128:
                                              ho + (ic + 1) * 128],
                                         b_wdq[:, ic * D + foh * G:
                                               ic * D + foh * G + G],
                                         start=ic == 0, stop=ic == IC0 - 1)
                        inc(i, sT, 1)
                    step(tensor=ft)
                    mmv = val(sT)

                    def fv2(v, foh=foh, pbk=pbk, xg=xg, mmv=mmv):
                        v.wait_ge(sT, mmv)
                        v.scalar_tensor_tensor(
                            b_do[:], pbk[:, 0:G], c_misc[:, 7:8],
                            c_swd_b[:, foh * G:(foh + 1) * G],
                            OP.mult, OP.mult)
                        vfence(v)
                        v.tensor_tensor(xg[:, foh * G:(foh + 1) * G],
                                        xg[:, foh * G:(foh + 1) * G],
                                        b_do[:], OP.add)
                        vfence(v)
                    step(vector=fv2)
                v_mark[hb] = val(sV)
                rdn = val(sV)
                step(sync=lambda y, tb=tb, xg=xg, rdn=rdn: [
                    y.wait_ge(sV, rdn),
                    dma(y, hrows(xdst, tb * 128, 128, 0, D), t3(xg, D, 1))][-1])
            barrier()

    def phase_l1attn(xsrc, xdst):
        for s in range(NSEQ):
            with nc.sbuf_tensor(f"a1_xnTh{s}", [128, FC * SEQ], F16) as xnTh1, \
                 nc.sbuf_tensor(f"a1_xnTl{s}", [128, FC * SEQ], F16) as xnTl1, \
                 nc.sbuf_tensor(f"a1_wq{s}", [128, 4 * FC * 64], F16) as wq_s, \
                 nc.sbuf_tensor(f"a1_wk{s}", [128, 4 * FC * 64], F16) as wk_s, \
                 nc.sbuf_tensor(f"a1_wv{s}", [128, 4 * FC * 64], F16) as wv_s, \
                 nc.sbuf_tensor(f"a1_K{s}", [64, 2 * SEQ], F16) as b_K, \
                 nc.sbuf_tensor(f"a1_Q{s}", [64, 2 * G], F16) as b_Q, \
                 nc.sbuf_tensor(f"a1_V{s}", [128, 16 * 65], F32) as b_V, \
                 nc.sbuf_tensor(f"a1_Vs{s}", [64, G], F32) as b_Vs, \
                 nc.sbuf_tensor(f"a1_P{s}", [128, 16 * 256], F32) as b_P, \
                 nc.sbuf_tensor(f"a1_o1{s}", [128, 16 * 64], F32) as b_o1, \
                 nc.sbuf_tensor(f"a1_o16{s}", [128, 2 * 1024], F16) as b_o16, \
                 nc.sbuf_tensor(f"a1_oT{s}", [64, 2 * SEQ], F16) as b_oTs:

                for g in range(NG_SEQ):
                    gi = s * NG_SEQ + g
                    xg = bx[g % 2]
                    step(sync=lambda y, gi=gi, xg=xg: [
                        y.wait_ge(sV, val(sV)),
                        dma(y, t3(xg, D, NT_G),
                            hrows(xsrc, gi * G, G, 0, D))][-1])
                    xld = val(sY)
                    rdy = ln_stats(xg, gi, xld)
                    for tt in range(NT_G):
                        xnT_tile(xg, gi, tt, xnTh1, xnTl1, SEQ,
                                 g * G + tt * 128, rdy)
                xnT_dn = val(sV)

                def wbuf(ws, half, buf):
                    off = (half * 2 + buf) * FC * 64
                    return ws[:, off:off + FC * 64]

                def load_head_w(y, h, buf):
                    for ws, nm in ((wq_s, "qw"), (wk_s, "kw"), (wv_s, "vw")):
                        for half in range(2):
                            dma(y, t3(wbuf(ws, half, buf), 64, FC),
                                hrows(wsplit[nm][half], 0, D, h * 64, 64))

                head_mark = {}
                oT_st = [0]
                step(sync=lambda y: load_head_w(y, 0, 0))

                for h in range(H1):
                    buf = h % 2

                    def fy2(y, h=h):
                        y.wait_ge(sT, head_mark.get(h - 1, 0))
                        if h + 1 < H1:
                            load_head_w(y, h + 1, (h + 1) % 2)
                    step(sync=fy2)
                    wrdy = val(sY)
                    step(vector=lambda v: [
                        v.wait_ge(sT, val(sT)),
                        v.memset(t3(b_V, 65, 16)[:, :, 64:65], 1.0),
                        vfence(v)][-1])
                    vones = val(sV)
                    exp_last = {0: 0, 1: 0}
                    nrm_last = [0]

                    for g in range(NG_SEQ):
                        def ftq(t, g=g, buf=buf, wrdy=wrdy, xnT_dn=xnT_dn):
                            t.wait_ge(sY, wrdy)
                            t.wait_ge(sV, val(sV) if (g or h) else xnT_dn)
                            for dst, ws in ((pb[0], wq_s), (pb[1], wk_s),
                                            (pb[2], wv_s)):
                                i = None
                                for fc in range(FC):
                                    i = mm3(t, dst[0:64, 0:G],
                                            wbuf(ws, 0, buf)[:, fc * 64:(fc + 1) * 64],
                                            wbuf(ws, 1, buf)[:, fc * 64:(fc + 1) * 64],
                                            xnTh1[:, fc * SEQ + g * G:
                                                  fc * SEQ + (g + 1) * G],
                                            xnTl1[:, fc * SEQ + g * G:
                                                  fc * SEQ + (g + 1) * G],
                                            fc == 0, fc == FC - 1)
                            inc(i, sT, 1)
                        step(tensor=ftq)
                        qkvm = val(sT)

                        def fve(v, g=g, qkvm=qkvm):
                            v.wait_ge(sT, qkvm)
                            v.tensor_copy(b_Q[:, 0:G], pb[0][0:64, 0:G])
                            v.tensor_tensor(b_Q[:, G:2 * G], pb[0][0:64, 0:G],
                                            b_Q[:, 0:G], OP.subtract)
                            v.tensor_copy(b_K[:, g * G:(g + 1) * G],
                                          pb[1][0:64, 0:G])
                            v.tensor_tensor(b_K[:, SEQ + g * G:SEQ + (g + 1) * G],
                                            pb[1][0:64, 0:G],
                                            b_K[:, g * G:(g + 1) * G],
                                            OP.subtract)
                            v.tensor_copy(b_Vs[:, 0:G], pb[2][0:64, 0:G])
                            vfence(v)
                        step(vector=fve)
                        vev = val(sV)

                        def ftv(t, vev=vev):
                            t.wait_ge(sV, vev)
                            i = None
                            for tt in range(NT_G):
                                i = t.transpose(
                                    pb[3][:, tt * 128:tt * 128 + 64],
                                    b_Vs[0:64, tt * 128:(tt + 1) * 128],
                                    c_id32[0:64, 0:64])
                            inc(i, sT, 1)
                        step(tensor=ftv)
                        vtm = val(sT)
                        step(vector=lambda v, g=g, vtm=vtm, vones=vones: [
                            v.wait_ge(sT, vtm),
                            v.wait_ge(sV, vones),
                            v.tensor_copy(
                                t3(b_V, 65, 16)[:, g * 4:(g + 1) * 4, 0:64],
                                t3(pb[3], 128, NT_G)[:, :, 0:64]),
                            vfence(v)][-1])
                        vdn = val(sV)

                        for j2 in range(2):
                            j = g * 2 + j2
                            last_pair = 0
                            for p in range(j + 1):
                                bs = p % 2
                                banks = (pb[4], pb[5]) if bs == 0 else \
                                        (pb[2], pb[3])

                                def fts(t, p=p, bs=bs, banks=banks, j2=j2,
                                        vdn=vdn, eg=exp_last[bs],
                                        nl=nrm_last[0]):
                                    t.wait_ge(sV, vdn)
                                    t.wait_ge(sS, eg)
                                    if bs == 1:
                                        t.wait_ge(sV, nl)
                                    for half in range(2):
                                        kt = 2 * p + half
                                        pbs = banks[half]
                                        t.matmul(pbs[:, 0:256],
                                                 b_K[:, kt * 128:
                                                     (kt + 1) * 128],
                                                 b_Q[:, j2 * 256:
                                                     (j2 + 1) * 256],
                                                 start=True, stop=False)
                                        t.matmul(pbs[:, 0:256],
                                                 b_K[:, kt * 128:
                                                     (kt + 1) * 128],
                                                 b_Q[:, G + j2 * 256:
                                                     G + (j2 + 1) * 256],
                                                 start=False, stop=False)
                                        i = t.matmul(
                                            pbs[:, 0:256],
                                            b_K[:, SEQ + kt * 128:
                                                SEQ + (kt + 1) * 128],
                                            b_Q[:, j2 * 256:(j2 + 1) * 256],
                                            start=False, stop=True)
                                    inc(i, sT, 1)
                                step(tensor=fts)
                                smm = val(sT)

                                def fexp(sc, p=p, banks=banks, smm=smm):
                                    sc.wait_ge(sT, smm)
                                    for half in range(2):
                                        kt = 2 * p + half
                                        i = sc.activation(
                                            b_P[:, kt * 256:(kt + 1) * 256],
                                            banks[half][:, 0:256], AF.Exp,
                                            scale=1.0 / 32.0,
                                            bias=c_eps[:, 1:2])
                                    inc(i, sS, 1)
                                step(scalar=fexp)
                                last_pair = exp_last[bs] = val(sS)
                            step(vector=lambda v, kt=2 * j, em=last_pair: [
                                v.wait_ge(sS, em),
                                v.tensor_tensor(
                                    b_P[:, kt * 256:kt * 256 + 128],
                                    b_P[:, kt * 256:kt * 256 + 128],
                                    c_mask128[:], OP.mult),
                                vfence(v)][-1])
                            step(vector=lambda v, kt=2 * j + 1, em=last_pair: [
                                v.wait_ge(sS, em),
                                v.memset(b_P[:, kt * 256:kt * 256 + 128],
                                         0.0),
                                v.tensor_tensor(
                                    b_P[:, kt * 256 + 128:(kt + 1) * 256],
                                    b_P[:, kt * 256 + 128:(kt + 1) * 256],
                                    c_mask128[:], OP.mult),
                                vfence(v)][-1])
                            pdn = val(sV)
                            sdn = val(sS)
                            def ftp(t, j=j, pdn=pdn, sdn=sdn):
                                t.wait_ge(sS, sdn)
                                t.wait_ge(sV, pdn)
                                i = None
                                for qh in range(2):
                                    jq = 2 * j + qh
                                    dst = pb[3] if qh == 0 else pb[2]
                                    for kt in range(jq + 1):
                                        i = t.matmul(
                                            dst[:, 0:65],
                                            b_P[:, kt * 256 + qh * 128:
                                                kt * 256 + qh * 128 + 128],
                                            b_V[:, kt * 65:(kt + 1) * 65],
                                            start=kt == 0, stop=kt == jq)
                                inc(i, sT, 1)
                            step(tensor=ftp)
                            pvm = val(sT)

                            def fnrm(v, j=j, pvm=pvm):
                                v.wait_ge(sT, pvm)
                                v.reciprocal(c_red[:, 40:41],
                                             pb[3][:, 64:65])
                                v.reciprocal(c_red[:, 48:49],
                                             pb[2][:, 64:65])
                                k = vfence(v)
                                v.wait_ge(sV, k)
                                v.tensor_scalar(
                                    b_o1[:, (2 * j) * 64:(2 * j + 1) * 64],
                                    pb[3][:, 0:64], c_red[:, 40:41], None,
                                    OP.mult)
                                inc(v.tensor_scalar(
                                    b_o1[:, (2 * j + 1) * 64:
                                         (2 * j + 2) * 64],
                                    pb[2][:, 0:64], c_red[:, 48:49], None,
                                    OP.mult), sV, 1)
                            step(vector=fnrm)
                            nrm_last[0] = val(sV)

                    odn = val(sV)
                    step(vector=lambda v, odn=odn: [
                        v.wait_ge(sV, odn),
                        vfence(v),
                        v.tensor_copy(b_o16[:, 0:1024], b_o1[:]),
                        v.tensor_tensor(b_o16[:, 1024:2048], b_o1[:],
                                        b_o16[:, 0:1024], OP.subtract),
                        vfence(v)][-1])
                    for half in range(2):
                        for bt in range(2):
                            def fto(t, half=half, bt=bt):
                                t.wait_ge(sV, val(sV))
                                i = None
                                for k in range(8):
                                    idx = bt * 8 + k
                                    i = t.transpose(
                                        pt16[0:64, k * 128:(k + 1) * 128],
                                        b_o16[:, half * 1024 + idx * 64:
                                              half * 1024 + (idx + 1) * 64],
                                        c_id16[:])
                                inc(i, sT, 1)
                            step(tensor=fto)
                            tdn = val(sT)
                            step(vector=lambda v, half=half, bt=bt, tdn=tdn,
                                 om=oT_st[0]: [
                                v.wait_ge(sT, tdn),
                                v.wait_ge(sY, om),
                                v.tensor_copy(
                                    b_oTs[:, half * SEQ + bt * 1024:
                                          half * SEQ + (bt + 1) * 1024],
                                    pt16[0:64, 0:1024]),
                                vfence(v)][-1])
                    stdn = val(sV)
                    step(sync=lambda y, h=h, stdn=stdn: [
                        y.wait_ge(sV, stdn),
                        dma(y, oT_d[0][h * 64:(h + 1) * 64, :],
                            b_oTs[:, 0:SEQ]),
                        dma(y, oT_d[1][h * 64:(h + 1) * 64, :],
                            b_oTs[:, SEQ:2 * SEQ])][-1])
                    oT_st[0] = val(sY)
                    head_mark[h] = val(sT)
                barrier()

            with nc.sbuf_tensor(f"a1_ow{s}", [128, 4 * FC * G], F16) as b_ow, \
                 nc.sbuf_tensor(f"a1_pTh{s}", [128, FC * G], F16) as b_pTh, \
                 nc.sbuf_tensor(f"a1_pTl{s}", [128, FC * G], F16) as b_pTl:

                def fow(y):
                    for hl in range(2):
                        for foh in range(2):
                            dma(y, t3(b_ow, G, FC, off=(hl * 2 + foh) * FC * G),
                                hrows(wsplit["ow"][hl], 0, D, foh * G, G))
                step(sync=fow)
                owld = val(sY)

                for g in range(NG_SEQ):
                    gi = s * NG_SEQ + g
                    xg = bx[g % 2]

                    def fy(y, g=g, gi=gi, xg=xg):
                        y.wait_ge(sV, val(sV))
                        y.wait_ge(sT, val(sT))
                        dma(y, t3(xg, D, NT_G), hrows(xsrc, gi * G, G, 0, D))
                        dma(y, t3(b_pTh, G, FC), hrows(oT_d[0], 0, D,
                                                       g * G, G))
                        dma(y, t3(b_pTl, G, FC), hrows(oT_d[1], 0, D,
                                                       g * G, G))
                    step(sync=fy)
                    old = val(sY)
                    for foh in range(2):
                        for tt in range(NT_G):
                            pbk = pb[tt % 2]

                            def fto2(t, foh=foh, tt=tt, pbk=pbk, old=old,
                                     owld=owld):
                                t.wait_ge(sY, max(old, owld))
                                t.wait_ge(sV, val(sV))
                                i = None
                                for fc in range(FC):
                                    i = mm3(t, pbk[:, 0:G],
                                            b_pTh[:, fc * G + tt * 128:
                                                  fc * G + (tt + 1) * 128],
                                            b_pTl[:, fc * G + tt * 128:
                                                  fc * G + (tt + 1) * 128],
                                            b_ow[:, (0 * 2 + foh) * FC * G +
                                                 fc * G:
                                                 (0 * 2 + foh) * FC * G +
                                                 (fc + 1) * G],
                                            b_ow[:, (1 * 2 + foh) * FC * G +
                                                 fc * G:
                                                 (1 * 2 + foh) * FC * G +
                                                 (fc + 1) * G],
                                            fc == 0, fc == FC - 1)
                                inc(i, sT, 1)
                            step(tensor=fto2)
                            omm = val(sT)
                            step(vector=lambda v, foh=foh, tt=tt, pbk=pbk,
                                 xg=xg, omm=omm: [
                                v.wait_ge(sT, omm),
                                v.tensor_tensor(
                                    xg[:, tt * D + foh * G:
                                       tt * D + (foh + 1) * G],
                                    xg[:, tt * D + foh * G:
                                       tt * D + (foh + 1) * G],
                                    pbk[:, 0:G], OP.add),
                                vfence(v)][-1])
                    rdn = val(sV)
                    step(sync=lambda y, gi=gi, xg=xg, rdn=rdn: [
                        y.wait_ge(sV, rdn),
                        dma(y, hrows(xdst, gi * G, G, 0, D),
                            t3(xg, D, NT_G))][-1])
                barrier()

    def phase_ffn1(xsrc, xdst):
        with nc.sbuf_tensor("f1_xnTh", [128, FC * G], F16) as xnTh4, \
             nc.sbuf_tensor("f1_xnTl", [128, FC * G], F16) as xnTl4, \
             nc.sbuf_tensor("f1_w1", [128, 4 * FC * 512], F16) as b_w1, \
             nc.sbuf_tensor("f1_hq", [128, IC1 * G], F8) as b_hq, \
             nc.sbuf_tensor("f1_w2", [128, IC1 * D], F8) as b_w2q, \
             nc.sbuf_tensor("f1_f16", [128, G], F16) as b_f16:

            step(sync=lambda y: dma(y, t3(b_w2q, D, IC1),
                                    hrows(w2q_d, 0, I1, 0, D)))
            w2ld = val(sY)

            def w1buf(half, sb_):
                return b_w1[:, (half * 2 + sb_) * FC * 512:
                            (half * 2 + sb_ + 1) * FC * 512]

            w1_mark = [0, 0]
            for gi in range(NG):
                xg = bx[gi % 2]
                step(sync=lambda y, gi=gi, xg=xg: [
                    y.wait_ge(sV, val(sV)),
                    dma(y, t3(xg, D, NT_G), hrows(xsrc, gi * G, G, 0, D))][-1])
                xld = val(sY)
                rdy = ln_stats(xg, gi, xld)
                for tt in range(NT_G):
                    xnT_tile(xg, gi, tt, xnTh4, xnTl4, G, tt * 128, rdy)
                xn_dn = val(sV)

                for sl in range(8):
                    sb_ = sl % 2

                    def fyw(y, sl=sl, sb_=sb_):
                        y.wait_ge(sT, w1_mark[sb_])
                        for half in range(2):
                            dma(y, t3(w1buf(half, sb_), 512, FC),
                                hrows(wsplit["w1"][half], 0, D,
                                      sl * 512, 512))
                    step(sync=fyw)
                    wld = val(sY)
                    for icl in range(4):
                        ic = sl * 4 + icl
                        pbk = pb[icl % 2]

                        def ft(t, ic=ic, icl=icl, sb_=sb_, pbk=pbk, wld=wld,
                               xn_dn=xn_dn):
                            t.wait_ge(sY, wld)
                            t.wait_ge(sV, xn_dn)
                            t.wait_ge(sS, val(sS))
                            i = None
                            for fc in range(FC):
                                i = mm3(t, pbk[:, 0:G],
                                        w1buf(0, sb_)[:, fc * 512 + icl * 128:
                                                      fc * 512 + (icl + 1) * 128],
                                        w1buf(1, sb_)[:, fc * 512 + icl * 128:
                                                      fc * 512 + (icl + 1) * 128],
                                        xnTh4[:, fc * G:(fc + 1) * G],
                                        xnTl4[:, fc * G:(fc + 1) * G],
                                        fc == 0, fc == FC - 1)
                            inc(i, sT, 1)
                        step(tensor=ft)
                        hmm = val(sT)
                        step(scalar=lambda sc, ic=ic, pbk=pbk, hmm=hmm: [
                            sc.wait_ge(sT, hmm),
                            inc(sc.activation(b_hq[:, ic * G:(ic + 1) * G],
                                              pbk[:, 0:G], AF.Relu,
                                              bias=c_b1[:, ic:ic + 1]),
                                sS, 1)][-1])
                    w1_mark[sb_] = val(sT)
                hq_dn = val(sS)

                for tt in range(NT_G):
                    for foh in range(2):
                        pbk = pb[2 + ((tt * 2 + foh) % 2)]

                        def ft2(t, tt=tt, foh=foh, pbk=pbk, hq_dn=hq_dn,
                                w2ld=w2ld):
                            t.wait_ge(sS, hq_dn)
                            t.wait_ge(sY, w2ld)
                            t.wait_ge(sS, val(sS))
                            for ic in range(IC1):
                                t.matmul(pbk[:, 0:G],
                                         b_hq[:, ic * G + tt * 128:
                                              ic * G + (tt + 1) * 128],
                                         b_w2q[:, ic * D + foh * G:
                                               ic * D + (foh + 1) * G],
                                         start=ic == 0, stop=False)
                            inc(t.matmul(pbk[:, 0:G], c_ones16[:],
                                         c_b2q[0:1, foh * G:(foh + 1) * G],
                                         start=False, stop=True), sT, 1)
                        step(tensor=ft2)
                        omm = val(sT)
                        step(scalar=lambda sc, pbk=pbk, omm=omm: [
                            sc.wait_ge(sT, omm),
                            sc.wait_ge(sV, val(sV)),
                            inc(sc.activation(b_f16[:], pbk[:, 0:G], AF.Copy),
                                sS, 1)][-1])
                        qdn = val(sS)
                        step(vector=lambda v, tt=tt, foh=foh, xg=xg, qdn=qdn: [
                            v.wait_ge(sS, qdn),
                            v.tensor_tensor(
                                xg[:, tt * D + foh * G:tt * D + (foh + 1) * G],
                                xg[:, tt * D + foh * G:tt * D + (foh + 1) * G],
                                b_f16[:], OP.add),
                            vfence(v)][-1])
                rdn = val(sV)
                step(sync=lambda y, gi=gi, xg=xg, rdn=rdn: [
                    y.wait_ge(sV, rdn),
                    dma(y, hrows(xdst, gi * G, G, 0, D), t3(xg, D, NT_G))][-1])
            barrier()

    def finish():
        def f(y):
            y.wait_ge(sY, val(sY))
            y.wait_ge(sG, val(sG))
        step(sync=f)

    phase0()
    if stage == 1:
        phase_l0attn(x_in, y_out)
    else:
        phase_l0attn(x_in, xb[0])
        if stage == 2:
            phase_ffn0(xb[0], y_out)
        else:
            phase_ffn0(xb[0], xb[1])
            if stage == 3:
                phase_l1attn(xb[1], y_out)
            else:
                phase_l1attn(xb[1], xb[0])
                phase_ffn1(xb[0], y_out)
    finish()
    stk.close()
    return nc



def _host_inputs(inputs):
    f32 = np.float32
    perm = np.arange(D)
    perm = (perm % 64) * 16 + perm // 64
    common = {
        "wqt0": np.ascontiguousarray(np.asarray(inputs["L0_attn_Wq"], f32).T),
        "wkt0": np.ascontiguousarray(np.asarray(inputs["L0_attn_Wk"], f32).T),
        "wvt0": np.ascontiguousarray(np.asarray(inputs["L0_attn_Wv"], f32).T),
        "wot0": np.ascontiguousarray(np.asarray(inputs["L0_attn_Wo"], f32).T),
        "fmqw": np.ascontiguousarray(np.asarray(inputs["L0_attn_fmq_w"], f32).T),
        "fmkw": np.ascontiguousarray(np.asarray(inputs["L0_attn_fmk_w"], f32).T),
        "fmqb": np.asarray(inputs["L0_attn_fmq_b"], f32),
        "fmkb": np.asarray(inputs["L0_attn_fmk_b"], f32),
        "rmsw": np.asarray(inputs["L0_attn_rms_w"], f32),
        "clipx": np.asarray(inputs["L0_ffn_clip_x"], f32),
        "clipd": np.asarray(inputs["L0_ffn_clip_down"], f32),
        "wg": np.asarray(inputs["L0_ffn_w_gate"], f32),
        "wu": np.asarray(inputs["L0_ffn_w_up"], f32),
        "wd": np.asarray(inputs["L0_ffn_w_down"], f32),
        "qwp": np.ascontiguousarray(np.asarray(inputs["L1_attn_qw"], f32)[:, perm]),
        "kwp": np.ascontiguousarray(np.asarray(inputs["L1_attn_kw"], f32)[:, perm]),
        "vwp": np.ascontiguousarray(np.asarray(inputs["L1_attn_vw"], f32)[:, perm]),
        "owp": np.ascontiguousarray(np.asarray(inputs["L1_attn_ow"], f32)[perm, :]),
        "w1t": np.ascontiguousarray(np.asarray(inputs["L1_ffn_w1"], f32).T),
        "b1": np.asarray(inputs["L1_ffn_b1"], f32),
        "w2t": np.ascontiguousarray(np.asarray(inputs["L1_ffn_w2"], f32).T),
        "b2": np.asarray(inputs["L1_ffn_b2"], f32),
        "id32": np.eye(128, dtype=f32),
        "id16": np.eye(128, dtype=np.float16),
        "idb16": np.eye(128, dtype=ml_dtypes.bfloat16),
        "mask64": np.vstack([np.triu(np.ones((64, 64), f32))] * 2),
        "mask128": np.triu(np.ones((128, 128), f32)),
        "ones16": np.ones((1, 128), np.float16),
        "ones32": np.ones((1, 128), f32),
    }
    x = np.asarray(inputs["L0_attn_x"], f32)
    maps = []
    for c in range(NCORES):
        m = dict(common)
        m["x"] = np.ascontiguousarray(x[2 * c:2 * c + 2].reshape(TOK, D))
        maps.append(m)
    return maps


def _run(stage, inputs, tap=None, cores=None):
    maps = _host_inputs(inputs)
    _TAP[0] = tap
    nc = build_program(stage=stage)
    _TAP[0] = None
    cores = cores if cores is not None else list(range(NCORES))
    res = run_bass_kernel_spmd(nc, [maps[c] for c in cores], core_ids=cores)
    y = np.stack(
        [np.asarray(r["y"]).reshape(NSEQ, SEQ, D) for r in res.results],
    ).reshape(len(cores) * NSEQ, SEQ, D)
    if tap is not None:
        return y, np.asarray(res.results[0]["dbg"])
    return y


def kernel(**inputs):
    return _run(4, inputs)
